# version 01
# baseline (speedup 1.0000x reference)
import math
import numpy as np
import ml_dtypes

import concourse.bass as bass
import concourse.mybir as mybir
from concourse.bass_utils import run_bass_kernel_spmd

F32 = mybir.dt.float32
BF16 = mybir.dt.bfloat16
AF = mybir.ActivationFunctionType
ALU = mybir.AluOpType
AXX = mybir.AxisListType.X

B, T, C, H, DH = 4, 4096, 1024, 16, 64
TC = 2048
NCC = 8
PBLK = 256
NPB = TC // PBLK
RCH = 128
NRC = TC // RCH
FBLK = 512
NFB = TC // FBLK
IF = 2816
NI = IF // 128
QMAX = 127.0
EPS = 1e-5
RMAGIC = 1.5 * 2.0 ** 23

_glo = math.log(1.0 - 0.999)
_ghi = math.log(1.0 - 0.9)
GAMMAS = [1.0 - math.exp(_glo + (_ghi - _glo) * h / (H - 1)) for h in range(H)]

DEBUG_TAPS = ()
PHASE_LIMIT = 99
SUB2 = 9
SUB3 = 3
SUB4 = 9

ENGS = ("pe", "act", "dve", "pool", "sp")


class _Op:
    __slots__ = ("eng", "fn", "sem", "amt", "needs_inc", "token", "waits", "gi",
                 "kind", "deps")

    def __init__(self, eng, fn, sem, amt, gi, kind):
        self.eng = eng
        self.fn = fn
        self.sem = sem
        self.amt = amt
        self.needs_inc = False
        self.token = None
        self.waits = ()
        self.gi = gi
        self.kind = kind
        self.deps = set()


class PB:
    NDMA = 4

    def __init__(self, nc):
        self.nc = nc
        self.ops = []
        self.kw = {}
        self.kr = {}
        self.pending = {eng: set() for eng in ENGS}
        self.sem_names = set()
        self.dma_idx = {}
        self.dma_last = {}

    def _dep_add(self, deps, op):
        if op is None:
            return
        lane = (op.eng, op.sem)
        cur = deps.get(lane)
        if cur is None or op.gi > cur.gi:
            deps[lane] = op

    def op(self, eng, fn, reads=(), writes=(), sem=None, amt=1, kind="c"):
        sem = sem or (("q" + eng) if kind == "dma" else ("e" + eng))
        o = _Op(eng, fn, sem, amt, len(self.ops), kind)
        deps = {}
        for k in reads:
            self._dep_add(deps, self.kw.get(k))
            if isinstance(k, tuple) and k and k[0] == "PS":
                for r in self.kr.get(k, {}).values():
                    if r.eng != eng:
                        self._dep_add(deps, r)
        for k in writes:
            self._dep_add(deps, self.kw.get(k))
            for r in self.kr.get(k, {}).values():
                self._dep_add(deps, r)
        for p in self.pending[eng]:
            self._dep_add(deps, p)
        self.pending[eng] = set()
        o.deps = set(deps.values())
        for k in reads:
            self.kr.setdefault(k, {})[(o.eng, o.sem)] = o
        for k in writes:
            self.kw[k] = o
            self.kr[k] = {}
        self.ops.append(o)
        return o

    def dma(self, queue, out_ap, in_ap, reads=(), writes=(), slow=False):
        def fn(e, o=out_ap, i=in_ap, s=slow):
            if s:
                return e.dma_start(out=o, in_=i, allow_slow_non_contiguous=True)
            return e.dma_start(out=o, in_=i)
        idx = self.dma_idx.get(queue, 0)
        self.dma_idx[queue] = idx + 1
        rot = idx % self.NDMA
        sem = f"q{queue}{rot}"
        o = self.op(queue, fn, reads=reads, writes=writes, sem=sem, amt=16,
                    kind="dma")
        prev = self.dma_last.get((queue, rot))
        if prev is not None:
            o.deps.add(prev)
        self.dma_last[(queue, rot)] = o
        return o

    def cc(self, ccname, kindstr, aluop, groups, in_ap, out_ap, reads, writes):
        def fn(e, ks=kindstr, op=aluop, g=groups, i=in_ap, o=out_ap):
            return e.collective_compute(ks, op, replica_groups=g, ins=[i], outs=[o])
        self.sem_names.add(ccname)
        return self.op("pool", fn, reads=reads, writes=writes, sem=ccname, amt=1,
                       kind="cc")

    def mm(self, bank, fn, reads, start, stop):
        o = _Op("pe", fn, "epe", 1, len(self.ops), "c")
        deps = {}
        for k in reads:
            self._dep_add(deps, self.kw.get(k))
        if start:
            for r in self.kr.get(bank, {}).values():
                if r.eng != "pe":
                    self._dep_add(deps, r)
            w = self.kw.get(bank)
            if w is not None and w.eng != "pe":
                self._dep_add(deps, w)
        for p in self.pending["pe"]:
            self._dep_add(deps, p)
        self.pending["pe"] = set()
        o.deps = set(deps.values())
        for k in reads:
            self.kr.setdefault(k, {})[(o.eng, o.sem)] = o
        if start:
            self.kr[bank] = {}
        if stop:
            self.kw[bank] = o
            self.kr[bank] = {}
        self.ops.append(o)
        return o

    def barrier(self):
        dmas = list(self.dma_last.values())
        bar_ops = []
        for eng in ENGS:
            o = self.op(eng, lambda e: e.drain())
            for p in dmas:
                o.deps.add(p)
            bar_ops.append(o)
        for eng in ENGS:
            self.pending[eng] = set(bar_ops)

    def finalize(self):
        for o in self.ops:
            for dd in o.deps:
                dd.needs_inc = True
        for o in self.ops:
            if o.kind in ("dma", "cc"):
                o.needs_inc = True
        counts = {}
        for o in self.ops:
            if o.needs_inc:
                counts[o.sem] = counts.get(o.sem, 0) + o.amt
                o.token = (o.sem, counts[o.sem])
        marks = {eng: {} for eng in ENGS}
        for o in self.ops:
            w = {}
            m = marks[o.eng]
            for dd in o.deps:
                s, v = dd.token
                if m.get(s, 0) < v and w.get(s, 0) < v:
                    w[s] = v
            for s, v in w.items():
                m[s] = v
            o.waits = sorted(w.items(), key=str)
        self.sem_names.update(counts.keys())
        return counts


def _consts():
    gam = np.array(GAMMAS, dtype=np.float64)
    u = np.arange(RCH)
    tt = np.arange(RCH)
    diff = tt[None, :] - u[:, None]
    dm = np.zeros((RCH, H * RCH), dtype=np.float64)
    for h in range(H):
        dm[:, h * RCH:(h + 1) * RCH] = np.where(
            diff >= 0, gam[h] ** np.maximum(diff, 0), 0.0)
    dm = np.asarray(dm.astype(ml_dtypes.bfloat16))
    gq2 = np.zeros((DH, H * RCH), dtype=np.float64)
    for h in range(H):
        gq2[:, h * RCH:(h + 1) * RCH] = gam[h] ** (tt[None, :] + 1)
    gk = np.zeros((RCH, C), dtype=np.float64)
    for h in range(H):
        gk[:, h * DH:(h + 1) * DH] = (gam[h] ** (RCH - 1 - u))[:, None]
    gk = np.asarray(gk.astype(ml_dtypes.bfloat16))
    gdec = np.zeros((DH, H * DH), dtype=np.float32)
    for h in range(H):
        gdec[:, h * DH:(h + 1) * DH] = gam[h] ** RCH
    selT = np.zeros((2, 128), dtype=np.float32)
    selT[0, :64] = 1.0
    selT[1, 64:] = 1.0
    return dict(
        dmask=dm,
        gq2=np.asarray(gq2.astype(ml_dtypes.bfloat16)),
        gk=gk, gdec=gdec,
        ident_bf=np.asarray(np.eye(128, dtype=ml_dtypes.bfloat16)),
        ident_f32=np.eye(128, dtype=np.float32),
        ones_col_bf=np.asarray(np.ones((128, 1), dtype=ml_dtypes.bfloat16)),
        one_row_bf=np.asarray(np.ones((1, 128), dtype=ml_dtypes.bfloat16)),
        one_row_f32=np.ones((1, 128), dtype=np.float32),
        selT=selT)


def _wl(wT):
    return np.ascontiguousarray(
        wT.reshape(8, 128, 1024).transpose(1, 0, 2).reshape(128, 8192))


def _host_prepare(inputs):
    x = np.asarray(inputs["L0_attn_x"], dtype=np.float32)
    con = _consts()
    shared = {
        "wq_l": _wl(np.asarray(inputs["L0_attn_wq"], np.float32).T),
        "wk_l": _wl(np.asarray(inputs["L0_attn_wk"], np.float32).T),
        "wv_l": _wl(np.asarray(inputs["L0_attn_wv"], np.float32).T),
        "wo_l": _wl(np.asarray(inputs["L0_attn_wo"], np.float32).T),
        "wmod_l": np.ascontiguousarray(
            np.asarray(inputs["L0_wmod"], np.float32)
            .reshape(8, 128, 48, 128).transpose(2, 1, 0, 3).reshape(48, 128, 1024)),
        "wg_nat": np.asarray(inputs["L0_ffn_w_gate"], np.float32),
        "wu_nat": np.asarray(inputs["L0_ffn_w_up"], np.float32),
        "wd_nat": np.asarray(inputs["L0_ffn_w_down"], np.float32),
        "gnw_v": np.ascontiguousarray(
            np.asarray(inputs["L0_attn_gn_w"], np.float32).reshape(8, 128).T),
        "gnb_v": np.ascontiguousarray(
            np.asarray(inputs["L0_attn_gn_b"], np.float32).reshape(8, 128).T),
        "clipx": np.asarray(inputs["L0_ffn_clip_x"], np.float32).reshape(1, 1),
        "clipd": np.asarray(inputs["L0_ffn_clip_down"], np.float32).reshape(1, 1),
    }
    shared.update(con)
    cmat = np.asarray(inputs["L0_c"], np.float32)
    in_maps = []
    for core in range(8):
        b, half = core // 2, core % 2
        m = dict(shared)
        m["xT"] = np.ascontiguousarray(x[b, half * TC:(half + 1) * TC, :].T)
        m["cvec"] = np.ascontiguousarray(cmat[b].reshape(8, 128).T)
        m["smask_send"] = np.full((64, 1), 1.0 if half == 0 else 0.0, np.float32)
        m["smask_recv"] = np.full((64, 1), 1.0 if half == 1 else 0.0, np.float32)
        in_maps.append(m)
    return in_maps


def build_program(debug_taps=(), phase_limit=99, sub2=9, sub3=3, sub4=9):
    nc = bass.Bass("TRN2", target_bir_lowering=False)

    d = {}
    def din(name, shape, dt=F32):
        d[name] = nc.dram_tensor(name, list(shape), dt, kind="ExternalInput")
        return d[name]

    din("xT", (C, TC))
    for n in ("wq_l", "wk_l", "wv_l", "wo_l"):
        din(n, (128, 8192))
    din("wmod_l", (48, 128, 1024))
    din("wg_nat", (IF, C)); din("wu_nat", (IF, C)); din("wd_nat", (C, IF))
    din("gnw_v", (128, 8)); din("gnb_v", (128, 8))
    din("clipx", (1, 1)); din("clipd", (1, 1))
    din("cvec", (128, 8))
    din("smask_send", (64, 1)); din("smask_recv", (64, 1))
    din("dmask", (128, H * RCH), BF16); din("gq2", (DH, H * RCH), BF16)
    din("gk", (RCH, C), BF16); din("gdec", (DH, H * DH))
    din("ident_bf", (128, 128), BF16); din("ident_f32", (128, 128))
    din("ones_col_bf", (128, 1), BF16); din("one_row_bf", (1, 128), BF16)
    din("one_row_f32", (1, 128))
    din("selT", (2, 128))

    out_d = nc.dram_tensor("out", [TC, C], F32, kind="ExternalOutput")
    dbg = {}
    for t in debug_taps:
        shp, dt = {"h": ([128, NCC * TC], BF16), "x2": ([128, NCC * TC], BF16),
                   "xq": ([128, NCC * TC], BF16), "s": ([128, NI * TC], BF16),
                   "mod": ([128, 48], F32), "state": ([DH, H * DH], F32),
                   "y": ([NPB, 128, NCC * PBLK], BF16)}[t]
        dbg[t] = nc.dram_tensor("dbg_" + t, shp, dt, kind="ExternalOutput")

    vtmd = nc.dram_tensor("vtmd", [NRC, 128, C], BF16)
    ktmd = nc.dram_tensor("ktmd", [NRC, 128, C], BF16)
    yd = nc.dram_tensor("yd", [NPB, 128, NCC * PBLK], BF16)
    ccs_in = nc.dram_tensor("ccs_in", [DH, H * DH], F32)
    ccs_out = nc.dram_tensor("ccs_out", [DH, H * DH], F32)
    gn_in = nc.dram_tensor("gn_in", [32, 1], F32)
    gn_out = nc.dram_tensor("gn_out", [32, 1], F32)
    mx1_in = nc.dram_tensor("mx1_in", [16, 1], F32); mx1_out = nc.dram_tensor("mx1_out", [16, 1], F32)
    mx2_in = nc.dram_tensor("mx2_in", [16, 1], F32); mx2_out = nc.dram_tensor("mx2_out", [16, 1], F32)

    SB = {}
    def sb(name, shape, dt, off):
        SB[name] = nc.alloc_sbuf_tensor_at(name, list(shape), dt, offset=off)
        return SB[name]

    SBASE = 18 * 1024
    LIM = SBASE + 190 * 1024

    o = SBASE
    x_sb = sb("x", (128, NCC * TC), BF16, o); o += 32768
    idb_sb = sb("idb", (128, 128), BF16, o); o += 256
    idf_sb = sb("idf", (128, 128), F32, o); o += 512
    onec_sb = sb("onec", (128, 1), BF16, o); o += 32
    oner_sb = sb("oner", (1, 128), BF16, o); o += 256
    onerf_sb = sb("onerf", (1, 128), F32, o); o += 512
    selT_sb = sb("selTc", (2, 128), F32, o); o += 512
    cvec_sb = sb("cvecc", (128, 8), F32, o); o += 32
    silc_sb = sb("silc", (128, 8), F32, o); o += 32
    gnw_sb = sb("gnwc", (128, 8), F32, o); o += 32
    gnb_sb = sb("gnbc", (128, 8), F32, o); o += 32
    mod_sb = sb("modv", (128, 48), F32, o); o += 192
    modt_sb = sb("modt", (128, 48), F32, o); o += 192
    gscr_sb = sb("gscr", (128, 24), F32, o); o += 96
    clipx_sb = sb("clipxs", (1, 1), F32, o); o += 32
    clipd_sb = sb("clipds", (1, 1), F32, o); o += 32
    smask_s_sb = sb("smask_s", (64, 1), F32, o); o += 32
    smask_r_sb = sb("smask_r", (64, 1), F32, o); o += 32
    gsum_sb = sb("gsum", (8, 2), F32, o); o += 32
    gsq_sb = sb("gsq", (8, 2), F32, o); o += 32
    gsum2_sb = sb("gsum2", (2, 8), F32, o); o += 32
    gsq2_sb = sb("gsq2", (2, 8), F32, o); o += 32
    ysum_sb = sb("ysum", (128, 64), F32, o); o += 256
    ysq_sb = sb("ysq", (128, 64), F32, o); o += 256
    ysum8_sb = sb("ysum8", (128, 8), F32, o); o += 32
    ysq8_sb = sb("ysq8", (128, 8), F32, o); o += 32
    Av_sb = sb("Av", (128, 8), F32, o); o += 32
    Bv_sb = sb("Bv", (128, 8), F32, o); o += 32
    mu8_sb = sb("mu8", (128, 8), F32, o); o += 32
    sq8_sb = sb("sq8", (128, 8), F32, o); o += 32
    hmax_sb = sb("hmax64", (128, 64), F32, o); o += 256
    hmax1_sb = sb("hmax1", (128, 1), F32, o); o += 32
    smax_sb = sb("smaxT", (128, NI * NFB), F32, o); o += 512
    smax1_sb = sb("smax1", (128, 1), F32, o); o += 32
    sc16_sb = sb("sc16", (16, 1), F32, o); o += 32
    sc16b_sb = sb("sc16b", (16, 1), F32, o); o += 32
    sA_sb = sb("sA", (1, 1), F32, o); o += 32
    sB_sb = sb("sB", (1, 1), F32, o); o += 32
    rbf_sb = sb("rbf", (1, 1), BF16, o); o += 32
    rB_sb = sb("rB", (128, 1), F32, o); o += 32
    qsxB_sb = sb("qsxB", (128, 1), F32, o); o += 32
    rs2B_sb = sb("rs2B", (128, 1), F32, o); o += 32
    qsdB_sb = sb("qsdB", (128, 1), F32, o); o += 32
    wmx_sb = sb("wmx", (128, 1), F32, o); o += 32
    wri_sb = sb("wri", (128, 1), F32, o); o += 32
    wsc_sb = sb("wsc", (128, 1), F32, o); o += 32
    sgv_sb = sb("sgv", (128, NI), F32, o); o += 128
    suv_sb = sb("suv", (128, NI), F32, o); o += 128
    sqc_sb = sb("sqc", (128, 1), F32, o); o += 32
    epsv_sb = sb("epsv", (128, 1), F32, o); o += 32
    R0_END = o
    assert R0_END <= SBASE + 40 * 1024, R0_END

    o = R0_END
    mrow_sb = sb("mrow", (1, TC), BF16, o); o += 4096
    rrow_sb = sb("rrow", (1, TC), BF16, o); o += 4096
    rows_sb = sb("rowst", (1, PBLK * 5), F32, o); o += 5120
    xsq_sb = [sb(f"xsq{i}", (128, PBLK), BF16, o + i * 512) for i in range(2)]; o += 1024
    vst_sb = [sb(f"vst{i}", (128, PBLK), F32, o + i * 1024) for i in range(2)]; o += 2048
    wst_sb = [sb(f"wst{i}", (128, PBLK), F32, o + i * 1024) for i in range(2)]; o += 2048
    xst_sb = sb("xst", (128, TC), F32, o); o += 8192
    dm_sb = sb("dm", (128, H * RCH), BF16, o); o += 4096
    gq2_sb = sb("gq2c", (DH, H * RCH), BF16, o); o += 4096
    gk_sb = sb("gkc", (RCH, C), BF16, o); o += 2048
    gdec_sb = sb("gdecc", (DH, H * DH), F32, o); o += 4096
    S_sb = sb("S", (DH, H * DH), F32, o); o += 4096
    Sr_sb = sb("Srecv", (DH, H * DH), F32, o); o += 4096
    Sbf_sb = sb("Sbf", (DH, H * DH), BF16, o); o += 2048
    h_sb = [sb(f"h{i}", (128, NCC * PBLK), BF16, o + i * 4096) for i in range(2)]; o += 8192
    qt_sb = sb("qt", (DH, H * PBLK), BF16, o); o += 8192
    kt_sb = sb("kt", (DH, H * PBLK), BF16, o); o += 8192
    ktm_sb = [sb(f"ktm{i}", (128, C), BF16, o + i * 2048) for i in range(2)]; o += 4096
    vtm_sb = [sb(f"vtm{i}", (128, C), BF16, o + i * 2048) for i in range(2)]; o += 4096
    stm_sb = [sb(f"stm{i}", (128, H * RCH), BF16, o + i * 4096) for i in range(2)]; o += 8192
    yb_sb = [sb(f"yb{i}", (128, NCC * PBLK), BF16, o + i * 4096) for i in range(2)]; o += 8192
    wA_sb = sb("wA", (128, 8192), BF16, o); o += 16384
    wB_sb = sb("wB", (128, 8192), BF16, o); o += 16384
    wf_sb = [sb(f"wf{i}", (128, 1024), F32, o + i * 4096) for i in range(2)]; o += 8192
    yl_sb = [sb(f"yl{i}", (128, NCC * PBLK), BF16, o + i * 4096) for i in range(2)]; o += 8192
    assert o <= LIM, ("attn", o)

    o = R0_END
    s_sb = sb("s", (128, NI * TC), BF16, o); o += NI * TC * 2
    FFN2 = o
    h2_sb = sb("h2", (128, NCC * TC), BF16, o); o += 32768
    wig_sb = [sb(f"wig{i}", (128, 1024), BF16, o + i * 2048) for i in range(2)]; o += 4096
    wiu_sb = [sb(f"wiu{i}", (128, 1024), BF16, o + i * 2048) for i in range(2)]; o += 4096
    wnat_sb = sb("wnat", (128, 1024), F32, o); o += 4096
    wq1_sb = sb("wq1", (128, 1024), F32, o); o += 4096
    wint_sb = sb("wint", (128, 1024), BF16, o); o += 2048
    gsl_sb = [sb(f"gsl{i}", (128, FBLK), BF16, o + i * 1024) for i in range(2)]; o += 2048
    assert o <= LIM, ("ffn-gateup", o)

    o = FFN2
    wd_sb = sb("wd_op", (128, NI * 1024), BF16, o); o += NI * 2048
    xc_sb = [sb(f"xc{i}", (128, 128), F32, o + i * 512) for i in range(2)]; o += 1024
    wdn_sb = sb("wdnat", (128, IF // 2), F32, o); o += 5632
    wdq1_sb = sb("wdq1", (128, IF // 2), F32, o); o += 5632
    wdi_sb = sb("wdint", (128, IF // 2), BF16, o); o += 2816
    outb_sb = sb("outb", (128, C), F32, o); o += 4096
    assert o <= LIM, ("down", o)

    PS = [nc.alloc_psum_tensor(f"pb{i}", [128, 512], F32) for i in range(6)]
    PSB = [nc.alloc_psum_tensor(f"pbb{i}", [128, 1024], BF16) for i in range(2)]
    def bank(i):
        return ("PS", i)
    def bankB(i):
        return ("PS", "B", i)

    pb = PB(nc)

    def t_mm(bk, out_ap, lhsT, rhs, reads, start, stop, is_tr=False):
        def fn(e, o2=out_ap, l=lhsT, r=rhs, s=start, t=stop, it=is_tr):
            return e.matmul(o2, l, r, start=s, stop=t, is_transpose=it)
        return pb.mm(bk, fn, reads, start, stop)

    def a_act(out_ap, in_ap, func, reads, writes, scale=1.0, bias=0.0, accum=None):
        def fn(e, o2=out_ap, i=in_ap, f=func, s=scale, b=bias, a=accum):
            if a is not None:
                return e.activation(o2, i, f, bias=b, scale=s, accum_out=a)
            return e.activation(o2, i, f, bias=b, scale=s)
        return pb.op("act", fn, reads=reads, writes=writes)

    def v_tt(eng, out_ap, a_ap, b_ap, aluop, reads, writes):
        def fn(e, o2=out_ap, a=a_ap, b=b_ap, op2=aluop):
            return e.tensor_tensor(o2, a, b, op2)
        return pb.op(eng, fn, reads=reads, writes=writes)

    def v_ts(eng, out_ap, in_ap, s1, s2, op0, op1, reads, writes):
        def fn(e, o2=out_ap, i=in_ap, a=s1, b=s2, p0=op0, p1=op1):
            if b is None:
                return e.tensor_single_scalar(o2, i, a, p0)
            return e.tensor_scalar(o2, i, a, b, p0, p1)
        return pb.op(eng, fn, reads=reads, writes=writes)

    def v_stt(eng, out_ap, in0, scal, in1, op0, op1, reads, writes):
        def fn(e, o2=out_ap, i0=in0, s=scal, i1=in1, p0=op0, p1=op1):
            return e.scalar_tensor_tensor(o2, i0, s, i1, p0, p1)
        return pb.op(eng, fn, reads=reads, writes=writes)

    def v_red(eng, out_ap, in_ap, aluop, reads, writes, absval=False):
        def fn(e, o2=out_ap, i=in_ap, op2=aluop, av=absval):
            return e.tensor_reduce(o2, i, AXX, op2, apply_absolute_value=av)
        return pb.op(eng, fn, reads=reads, writes=writes)

    def v_copy(eng, out_ap, in_ap, reads, writes):
        def fn(e, o2=out_ap, i=in_ap):
            return e.tensor_copy(o2, i)
        return pb.op(eng, fn, reads=reads, writes=writes)

    def v_recip(eng, out_ap, in_ap, reads, writes):
        def fn(e, o2=out_ap, i=in_ap):
            return e.reciprocal(o2, i)
        return pb.op(eng, fn, reads=reads, writes=writes)

    def v_memset(eng, ap, val, writes):
        def fn(e, a=ap, v=val):
            return e.memset(a, v)
        return pb.op(eng, fn, reads=(), writes=writes)

    out_keys = []

    v_memset("dve", epsv_sb[:, :], EPS, writes=[("epsv",)])
    for nme, dst in (("dmask", dm_sb), ("gq2", gq2_sb), ("gk", gk_sb),
                     ("gdec", gdec_sb), ("ident_bf", idb_sb), ("ident_f32", idf_sb),
                     ("ones_col_bf", onec_sb), ("one_row_bf", oner_sb),
                     ("one_row_f32", onerf_sb),
                     ("selT", selT_sb), ("cvec", cvec_sb),
                     ("gnw_v", gnw_sb), ("gnb_v", gnb_sb),
                     ("clipx", clipx_sb), ("clipd", clipd_sb),
                     ("smask_send", smask_s_sb), ("smask_recv", smask_r_sb)):
        pb.dma("pool", dst[:, :], d[nme][:, :], reads=(), writes=[("c", nme)])

    for ccn in range(NCC):
        pb.dma("sp", xst_sb[:, :], d["xT"][ccn * 128:(ccn + 1) * 128, :],
               reads=[("xst",)], writes=[("xst",)])
        v_copy("dve", x_sb[:, ccn * TC:(ccn + 1) * TC], xst_sb[:, :],
               reads=[("xst",)], writes=[("x", ccn)])

    def load_weight(slot, src_name, slotkey):
        for ccn in range(NCC):
            st = wf_sb[ccn % 2]
            pb.dma("sp", st[:, :], d[src_name][:, ccn * 1024:(ccn + 1) * 1024],
                   reads=[("wf", ccn % 2)], writes=[("wf", ccn % 2)])
            v_copy("dve", slot[:, ccn * 1024:(ccn + 1) * 1024], st[:, :],
                   reads=[("wf", ccn % 2)], writes=[(slotkey, ccn)])

    load_weight(wA_sb, "wk_l", "wA")
    load_weight(wB_sb, "wv_l", "wB")

    a_act(silc_sb[:, :], cvec_sb[:, :], AF.Silu, reads=[("c", "cvec")],
          writes=[("silc",)])
    for i in range(48):
        st = wf_sb[i % 2]
        pb.dma("sp", st[:, :], d["wmod_l"][i, :, :],
               reads=[("wf", i % 2)], writes=[("wf", i % 2)])
        for j in range(NCC):
            t_mm(bank(5), PS[5][:, i:i + 1], st[:, j * 128:(j + 1) * 128],
                 silc_sb[:, j:j + 1],
                 reads=[("wf", i % 2), ("silc",)], start=(j == 0), stop=(j == 7))
    v_copy("dve", mod_sb[:, :], PS[5][:, 0:48], reads=[bank(5)], writes=[("modv",)])
    v_copy("dve", modt_sb[:, 0:8], mod_sb[:, 0:8], reads=[("modv",)], writes=[("mt", 0)])
    v_ts("dve", modt_sb[:, 8:16], mod_sb[:, 8:16], 1.0, None, ALU.add, None,
         reads=[("modv",)], writes=[("mt", 1)])
    v_copy("dve", modt_sb[:, 24:32], mod_sb[:, 24:32], reads=[("modv",)], writes=[("mt", 3)])
    v_ts("dve", modt_sb[:, 32:40], mod_sb[:, 32:40], 1.0, None, ALU.add, None,
         reads=[("modv",)], writes=[("mt", 4)])
    for (c0, mi) in ((16, 2), (40, 5)):
        a_act(gscr_sb[:, 0:8], mod_sb[:, c0:c0 + 8], AF.Sign,
              reads=[("modv",), ("gscr",)], writes=[("gscr",)])
        v_tt("dve", gscr_sb[:, 8:16], gscr_sb[:, 0:8], gscr_sb[:, 0:8], ALU.mult,
             reads=[("gscr",)], writes=[("gscr",)])
        v_ts("dve", gscr_sb[:, 8:16], gscr_sb[:, 8:16], -1.0, 1.0, ALU.mult, ALU.add,
             reads=[("gscr",)], writes=[("gscr",)])
        v_tt("dve", gscr_sb[:, 8:16], gscr_sb[:, 0:8], gscr_sb[:, 8:16], ALU.add,
             reads=[("gscr",)], writes=[("gscr",)])
        a_act(gscr_sb[:, 16:24], mod_sb[:, c0:c0 + 8], AF.Abs,
              reads=[("modv",), ("gscr",)], writes=[("gscr",)])
        v_ts("dve", gscr_sb[:, 16:24], gscr_sb[:, 16:24], 0.25, None, ALU.max, None,
             reads=[("gscr",)], writes=[("gscr",)])
        v_tt("dve", modt_sb[:, c0:c0 + 8], gscr_sb[:, 8:16], gscr_sb[:, 16:24],
             ALU.mult, reads=[("gscr",)], writes=[("mt", mi)])

    sa_c = modt_sb[:, 0:8]; a1_c = modt_sb[:, 8:16]; ga_c = modt_sb[:, 16:24]
    sb_c = modt_sb[:, 24:32]; a2_c = modt_sb[:, 32:40]; gb_c = modt_sb[:, 40:48]
    MT = [("mt", i) for i in range(6)]

    if "mod" in dbg:
        pb.dma("pool", dbg["mod"][:, :], modt_sb[:, :], reads=MT, writes=[("dbgmod",)])
        out_keys.append(("dbgmod",))

    def ln_stats(src_keyf):
        for blk in range(NPB):
            t0 = blk * PBLK
            for ccn in range(NCC):
                xs = x_sb[:, ccn * TC + t0: ccn * TC + t0 + PBLK]
                t_mm(bank(4), PS[4][:1, 0:PBLK], onec_sb[:, :], xs,
                     reads=[("c", "ones_col_bf"), src_keyf(ccn, blk)],
                     start=(ccn == 0), stop=(ccn == 7))
            for ccn in range(NCC):
                xs = x_sb[:, ccn * TC + t0: ccn * TC + t0 + PBLK]
                sqt = xsq_sb[ccn % 2]
                a_act(sqt[:, :], xs, AF.Square,
                      reads=[src_keyf(ccn, blk), ("xsq", ccn % 2)],
                      writes=[("xsq", ccn % 2)])
                t_mm(bank(5), PS[5][:1, 0:PBLK], onec_sb[:, :], sqt[:, :],
                     reads=[("c", "ones_col_bf"), ("xsq", ccn % 2)],
                     start=(ccn == 0), stop=(ccn == 7))
            rw = rows_sb
            P = PBLK
            v_ts("dve", rw[:1, 0:P], PS[4][:1, 0:P], 1.0 / C, None, ALU.mult, None,
                 reads=[bank(4), ("rows",)], writes=[("rows",)])
            v_ts("dve", rw[:1, P:2 * P], PS[5][:1, 0:P], 1.0 / C, None, ALU.mult, None,
                 reads=[bank(5), ("rows",)], writes=[("rows",)])
            v_tt("dve", rw[:1, 2 * P:3 * P], rw[:1, 0:P], rw[:1, 0:P], ALU.mult,
                 reads=[("rows",)], writes=[("rows",)])
            v_tt("dve", rw[:1, 2 * P:3 * P], rw[:1, P:2 * P], rw[:1, 2 * P:3 * P],
                 ALU.subtract, reads=[("rows",)], writes=[("rows",)])
            a_act(rw[:1, 3 * P:4 * P], rw[:1, 2 * P:3 * P], AF.Sqrt,
                  reads=[("rows",), ("epsv",)], writes=[("rows",)],
                  bias=epsv_sb[0:1, 0:1])
            v_recip("dve", rw[:1, 4 * P:5 * P], rw[:1, 3 * P:4 * P],
                    reads=[("rows",)], writes=[("rows",)])
            v_copy("dve", mrow_sb[:1, t0:t0 + P], rw[:1, 0:P],
                   reads=[("rows",)], writes=[("mrow", blk)])
            v_copy("dve", rrow_sb[:1, t0:t0 + P], rw[:1, 4 * P:5 * P],
                   reads=[("rows",)], writes=[("rrow", blk)])

    def bcast_rows(blk):
        t0 = blk * PBLK
        t_mm(bank(4), PS[4][:, 0:PBLK], oner_sb[:, :], mrow_sb[:1, t0:t0 + PBLK],
             reads=[("c", "one_row_bf"), ("mrow", blk)], start=True, stop=False)
        t_mm(bank(4), PS[4][:, PBLK:2 * PBLK], oner_sb[:, :], rrow_sb[:1, t0:t0 + PBLK],
             reads=[("c", "one_row_bf"), ("rrow", blk)], start=False, stop=True)

    def h_block(blk, srckeyf, dstf, dkeyf, a_vec, s_vec, mkeys):
        t0 = blk * PBLK
        bcast_rows(blk)
        for ccn in range(NCC):
            xs = x_sb[:, ccn * TC + t0: ccn * TC + t0 + PBLK]
            vt = vst_sb[ccn % 2]
            wt = wst_sb[ccn % 2]
            v_tt("dve", vt[:, :], xs, PS[4][:, 0:PBLK], ALU.subtract,
                 reads=[srckeyf(ccn, blk), bank(4), ("vst", ccn % 2)],
                 writes=[("vst", ccn % 2)])
            v_tt("dve", wt[:, :], vt[:, :], PS[4][:, PBLK:2 * PBLK], ALU.mult,
                 reads=[("vst", ccn % 2), bank(4), ("wst", ccn % 2)],
                 writes=[("wst", ccn % 2)])
            v_ts("dve", dstf(ccn, blk), wt[:, :], a_vec[:, ccn:ccn + 1],
                 s_vec[:, ccn:ccn + 1], ALU.mult, ALU.add,
                 reads=[("wst", ccn % 2)] + mkeys + [dkeyf(ccn, blk)],
                 writes=[dkeyf(ccn, blk)])

    def xkey(ccn, blk):
        return ("x", ccn)
    ln_stats(xkey)

    def hsl(ccn, blk):
        return h_sb[blk % 2][:, ccn * PBLK:(ccn + 1) * PBLK]

    def hkey(ccn, blk):
        return ("h", blk % 2, ccn)

    v_memset("dve", S_sb[:, :], 0.0, writes=[("S",)])

    for blk in range(NPB):
        h_block(blk, xkey, hsl, hkey, a1_c, sa_c, [("mt", 1), ("mt", 0)])
        for tci in range(2):
            ch = blk * 2 + tci
            kbuf = ktm_sb[ch % 2]
            vbuf = vtm_sb[ch % 2]
            for half in range(2):
                bk = bank(half)
                for ccn in range(NCC):
                    t_mm(bk, PS[half][:, 0:512],
                         hsl(ccn, blk)[:, tci * 128:(tci + 1) * 128],
                         wA_sb[:, ccn * 1024 + half * 512: ccn * 1024 + (half + 1) * 512],
                         reads=[hkey(ccn, blk), ("wA", ccn)],
                         start=(ccn == 0), stop=(ccn == 7))
                v_tt("dve", kbuf[:, half * 512:(half + 1) * 512], PS[half][:, 0:512],
                     gk_sb[:, half * 512:(half + 1) * 512], ALU.mult,
                     reads=[bk, ("c", "gk"), ("ktm", ch % 2)],
                     writes=[("ktm", ch % 2)])
                bkv = bank(2 + half)
                for ccn in range(NCC):
                    t_mm(bkv, PS[2 + half][:, 0:512],
                         hsl(ccn, blk)[:, tci * 128:(tci + 1) * 128],
                         wB_sb[:, ccn * 1024 + half * 512: ccn * 1024 + (half + 1) * 512],
                         reads=[hkey(ccn, blk), ("wB", ccn)],
                         start=(ccn == 0), stop=(ccn == 7))
                a_act(vbuf[:, half * 512:(half + 1) * 512], PS[2 + half][:, 0:512],
                      AF.Copy, reads=[bkv, ("vtm", ch % 2)], writes=[("vtm", ch % 2)])
            pb.dma("act", ktmd[ch, :, :], kbuf[:, :],
                   reads=[("ktm", ch % 2)], writes=[("ktmd", ch)])
            pb.dma("act", vtmd[ch, :, :], vbuf[:, :],
                   reads=[("vtm", ch % 2)], writes=[("vtmd", ch)])
            for h in range(H):
                hb = 4 if h < 8 else 5
                t_mm(bank(hb), PS[hb][0:DH, (h % 8) * DH:(h % 8 + 1) * DH],
                     kbuf[:, h * DH:(h + 1) * DH], vbuf[:, h * DH:(h + 1) * DH],
                     reads=[("ktm", ch % 2), ("vtm", ch % 2)],
                     start=(h % 8 == 0), stop=(h % 8 == 7))
            v_tt("dve", Sr_sb[:, :], S_sb[:, :], gdec_sb[:, :], ALU.mult,
                 reads=[("S",), ("c", "gdec"), ("Srecv",)], writes=[("Srecv",)])
            v_tt("dve", S_sb[:, 0:512], Sr_sb[:, 0:512], PS[4][0:DH, 0:512], ALU.add,
                 reads=[("Srecv",), bank(4), ("S",)], writes=[("S",)])
            v_tt("dve", S_sb[:, 512:1024], Sr_sb[:, 512:1024], PS[5][0:DH, 0:512],
                 ALU.add, reads=[("Srecv",), bank(5), ("S",)], writes=[("S",)])

    v_ts("dve", Sr_sb[:, :], S_sb[:, :], smask_s_sb[:, 0:1], None, ALU.mult, None,
         reads=[("S",), ("c", "smask_send"), ("Srecv",)], writes=[("Srecv",)])
    pb.dma("pool", ccs_in.ap(), Sr_sb[:, :], reads=[("Srecv",)], writes=[("ccsin",)])
    pb.cc("ccstate", "AllReduce", ALU.add, [[0, 1], [2, 3], [4, 5], [6, 7]],
          ccs_in.ap().opt(), ccs_out.ap().opt(),
          reads=[("ccsin",)], writes=[("ccsout",)])
    pb.dma("pool", Sr_sb[:, :], ccs_out.ap(), reads=[("ccsout",), ("Srecv",)],
           writes=[("Srecv",)])
    v_ts("dve", S_sb[:, :], Sr_sb[:, :], smask_r_sb[:, 0:1], None, ALU.mult, None,
         reads=[("Srecv",), ("c", "smask_recv"), ("S",)], writes=[("S",)])
    v_copy("dve", Sbf_sb[:, :], S_sb[:, :], reads=[("S",), ("Sbf",)],
           writes=[("Sbf",)])

    if "state" in dbg:
        pb.dma("pool", dbg["state"][:, :], S_sb[:, :], reads=[("S",)],
               writes=[("dbgS",)])
        out_keys.append(("dbgS",))

    if phase_limit >= 2:
        load_weight(wA_sb, "wq_l", "wA")
        load_weight(wB_sb, "wk_l", "wB")

        def fetch_kv(ch):
            pb.dma("act", ktm_sb[ch % 2][:, :], ktmd[ch, :, :],
                   reads=[("ktmd", ch), ("ktm", ch % 2)], writes=[("ktm", ch % 2)])
            pb.dma("act", vtm_sb[ch % 2][:, :], vtmd[ch, :, :],
                   reads=[("vtmd", ch), ("vtm", ch % 2)], writes=[("vtm", ch % 2)])

        fetch_kv(0)
        fetch_kv(1)

        for blk in range(NPB):
            h_block(blk, xkey, hsl, hkey, a1_c, sa_c, [("mt", 1), ("mt", 0)])
            for (slot, skey, dst, dkey) in ((wA_sb, "wA", qt_sb, "qt"),
                                            (wB_sb, "wB", kt_sb, "kt")):
                for ecn in range(NCC):
                    bk = bank(ecn % 2)
                    for ccn in range(NCC):
                        t_mm(bk, PS[ecn % 2][:, 0:PBLK],
                             slot[:, ccn * 1024 + ecn * 128: ccn * 1024 + (ecn + 1) * 128],
                             hsl(ccn, blk),
                             reads=[(skey, ccn), hkey(ccn, blk)],
                             start=(ccn == 0), stop=(ccn == 7))
                    for sub in range(2):
                        hh = 2 * ecn + sub
                        a_act(dst[:DH, hh * PBLK:(hh + 1) * PBLK],
                              PS[ecn % 2][sub * 64:(sub + 1) * 64, 0:PBLK], AF.Copy,
                              reads=[bk, (dkey, hh)], writes=[(dkey, hh)])
            for tci in range(2 if sub2 >= 2 else 0):
                ch = blk * 2 + tci
                kbuf = ktm_sb[ch % 2]
                vbuf = vtm_sb[ch % 2]
                smbuf = stm_sb[ch % 2]
                for g in range(4):
                    bks = bank(2 + g % 2)
                    for hi in range(4):
                        hh = 4 * g + hi
                        sl = slice(hh * PBLK + tci * 128, hh * PBLK + tci * 128 + 128)
                        t_mm(bks, PS[2 + g % 2][:, hi * 128:(hi + 1) * 128],
                             kt_sb[:DH, sl], qt_sb[:DH, sl],
                             reads=[("kt", hh), ("qt", hh)],
                             start=(hi == 0), stop=(hi == 3))
                    v_tt("dve", smbuf[:, g * 512:(g + 1) * 512],
                         PS[2 + g % 2][:, 0:512], dm_sb[:, g * 512:(g + 1) * 512],
                         ALU.mult,
                         reads=[bks, ("c", "dmask"), ("stm", ch % 2)],
                         writes=[("stm", ch % 2)])
                qv = qt_sb[:DH, :].rearrange("p (h t) -> p h t", h=H)[:, :, tci * 128:(tci + 1) * 128]
                gv = gq2_sb[:DH, :].rearrange("p (h t) -> p h t", h=H)
                v_tt("dve", qv, qv, gv, ALU.mult,
                     reads=[("c", "gq2")] + [("qt", hh) for hh in range(H)],
                     writes=[("qt", hh) for hh in range(H)])
                for g in range(4 if sub2 >= 3 else 0):
                    bky = bank(4 + g % 2)
                    for hi in range(4):
                        hh = 4 * g + hi
                        t_mm(bky, PS[4 + g % 2][0:DH, hi * 128:(hi + 1) * 128],
                             vbuf[:, hh * DH:(hh + 1) * DH],
                             smbuf[:, (g * 4 + hi) * 128:(g * 4 + hi + 1) * 128],
                             reads=[("vtm", ch % 2), ("stm", ch % 2)],
                             start=(hi == 0), stop=(False if sub3 >= 2 else hi == 3))
                    if sub3 >= 2:
                        for hi in range(4):
                            hh = 4 * g + hi
                            sl = slice(hh * PBLK + tci * 128, hh * PBLK + tci * 128 + 128)
                            t_mm(bky, PS[4 + g % 2][0:DH, hi * 128:(hi + 1) * 128],
                                 Sbf_sb[:DH, hh * DH:(hh + 1) * DH], qt_sb[:DH, sl],
                                 reads=[("Sbf",), ("qt", hh)],
                                 start=False, stop=(hi == 3))
                    for hi in range(4):
                        hh = 4 * g + hi
                        ccn = hh // 2
                        r0 = (hh % 2) * 64 if sub3 >= 3 else 0
                        a_act(yb_sb[blk % 2][r0:r0 + 64,
                                             ccn * PBLK + tci * 128: ccn * PBLK + tci * 128 + 128],
                              PS[4 + g % 2][0:DH, hi * 128:(hi + 1) * 128], AF.Copy,
                              reads=[bky, ("yb", blk % 2)], writes=[("yb", blk % 2)])
                if sub2 >= 4:
                    for h in range(H):
                        hb = 0 if h < 8 else 1
                        t_mm(bank(hb), PS[hb][0:DH, (h % 8) * DH:(h % 8 + 1) * DH],
                             kbuf[:, h * DH:(h + 1) * DH], vbuf[:, h * DH:(h + 1) * DH],
                             reads=[("ktm", ch % 2), ("vtm", ch % 2)],
                             start=(h % 8 == 0), stop=(h % 8 == 7))
                    v_tt("dve", Sr_sb[:, :], S_sb[:, :], gdec_sb[:, :], ALU.mult,
                         reads=[("S",), ("c", "gdec"), ("Srecv",)], writes=[("Srecv",)])
                    v_tt("dve", S_sb[:, 0:512], Sr_sb[:, 0:512], PS[0][0:DH, 0:512],
                         ALU.add, reads=[("Srecv",), bank(0), ("S",)], writes=[("S",)])
                    v_tt("dve", S_sb[:, 512:1024], Sr_sb[:, 512:1024], PS[1][0:DH, 0:512],
                         ALU.add, reads=[("Srecv",), bank(1), ("S",)], writes=[("S",)])
                    v_copy("dve", Sbf_sb[:, :], S_sb[:, :], reads=[("S",), ("Sbf",)],
                           writes=[("Sbf",)])
                if ch + 2 < NRC:
                    fetch_kv(ch + 2)
            for ccn in range(NCC if sub2 >= 5 else 0):
                ys = yb_sb[blk % 2][:, ccn * PBLK:(ccn + 1) * PBLK]
                col = blk * 8 + ccn
                v_red("dve", ysum_sb[:, col:col + 1], ys, ALU.add,
                      reads=[("yb", blk % 2)], writes=[("ysum", col)])
                v_tt("dve", xsq_sb[0][:, 0:PBLK], ys, ys, ALU.mult,
                     reads=[("yb", blk % 2), ("xsq", 0)], writes=[("xsq", 0)])
                v_red("dve", ysq_sb[:, col:col + 1], xsq_sb[0][:, 0:PBLK], ALU.add,
                      reads=[("xsq", 0)], writes=[("ysq", col)])
            if sub2 >= 3:
                pb.dma("act", yd[blk, :, :], yb_sb[blk % 2][:, :],
                       reads=[("yb", blk % 2)], writes=[("yd", blk)])

        if sub2 < 6:
            pb.dma("sp", out_d[0:128, :], xst_sb[:, 0:1024],
                   reads=[("xst",)], writes=[("outz",)])
            pb.op("sp", lambda e: e.nop(), reads=[("outz",)], writes=())
            pb.finalize()
            _emit(nc, pb)
            return nc
        ysr = ysum_sb[:, :].rearrange("p (a b) -> p b a", a=NPB)
        v_red("dve", ysum8_sb[:, :], ysr, ALU.add,
              reads=[("ysum", c2) for c2 in range(64)], writes=[("ysum8",)])
        ysr2 = ysq_sb[:, :].rearrange("p (a b) -> p b a", a=NPB)
        v_red("dve", ysq8_sb[:, :], ysr2, ALU.add,
              reads=[("ysq", c2) for c2 in range(64)], writes=[("ysq8",)])
        t_mm(bank(2), PS[2][0:8, 0:128], ysum8_sb[:, :], idf_sb[:, :],
             reads=[("ysum8",), ("c", "ident_f32")], start=True, stop=True, is_tr=True)
        t_mm(bank(3), PS[3][0:8, 0:128], ysq8_sb[:, :], idf_sb[:, :],
             reads=[("ysq8",), ("c", "ident_f32")], start=True, stop=True, is_tr=True)
        v_red("dve", gsum_sb[:, :],
              PS[2][0:8, 0:128].rearrange("p (h e) -> p h e", h=2), ALU.add,
              reads=[bank(2)], writes=[("gsum",)])
        v_red("dve", gsq_sb[:, :],
              PS[3][0:8, 0:128].rearrange("p (h e) -> p h e", h=2), ALU.add,
              reads=[bank(3)], writes=[("gsq",)])
        pb.dma("pool", gn_in[0:16, :], gsum_sb[:, :], reads=[("gsum",)],
               writes=[("gnin",)], slow=True)
        pb.dma("pool", gn_in[16:32, :], gsq_sb[:, :], reads=[("gsq",), ("gnin",)],
               writes=[("gnin",)], slow=True)
        pb.cc("ccgn", "AllReduce", ALU.add, [[0, 1], [2, 3], [4, 5], [6, 7]],
              gn_in.ap().opt(), gn_out.ap().opt(),
              reads=[("gnin",)], writes=[("gnout",)])
        pb.dma("pool", gsum2_sb[:, :],
               gn_out[0:16, :].rearrange("(a b) c -> b (a c)", b=2),
               reads=[("gnout",)], writes=[("gsum2",)], slow=True)
        pb.dma("pool", gsq2_sb[:, :],
               gn_out[16:32, :].rearrange("(a b) c -> b (a c)", b=2),
               reads=[("gnout",)], writes=[("gsq2",)], slow=True)
        for ccn in range(NCC):
            t_mm(bank(2), PS[2][:, ccn:ccn + 1], selT_sb[:, :], gsum2_sb[:, ccn:ccn + 1],
                 reads=[("c", "selT"), ("gsum2",)], start=(ccn == 0), stop=(ccn == 7))
            t_mm(bank(3), PS[3][:, ccn:ccn + 1], selT_sb[:, :], gsq2_sb[:, ccn:ccn + 1],
                 reads=[("c", "selT"), ("gsq2",)], start=(ccn == 0), stop=(ccn == 7))
        NTOT = float(T * DH)
        v_ts("dve", mu8_sb[:, :], PS[2][:, 0:8], 1.0 / NTOT, None, ALU.mult, None,
             reads=[bank(2)], writes=[("mu8",)])
        v_ts("dve", sq8_sb[:, :], PS[3][:, 0:8], 1.0 / NTOT, None, ALU.mult, None,
             reads=[bank(3)], writes=[("sq8",)])
        v_tt("dve", Av_sb[:, :], mu8_sb[:, :], mu8_sb[:, :], ALU.mult,
             reads=[("mu8",), ("Av",)], writes=[("Av",)])
        v_tt("dve", Av_sb[:, :], sq8_sb[:, :], Av_sb[:, :], ALU.subtract,
             reads=[("sq8",), ("Av",)], writes=[("Av",)])
        a_act(Bv_sb[:, :], Av_sb[:, :], AF.Sqrt, reads=[("Av",), ("Bv",), ("epsv",)],
              writes=[("Bv",)], bias=epsv_sb[:, 0:1])
        v_recip("dve", Av_sb[:, :], Bv_sb[:, :], reads=[("Bv",), ("Av",)],
                writes=[("Av",)])
        v_tt("dve", Av_sb[:, :], Av_sb[:, :], gnw_sb[:, :], ALU.mult,
             reads=[("Av",), ("c", "gnw_v")], writes=[("Av",)])
        v_tt("dve", Bv_sb[:, :], mu8_sb[:, :], Av_sb[:, :], ALU.mult,
             reads=[("mu8",), ("Av",), ("Bv",)], writes=[("Bv",)])
        v_tt("dve", Bv_sb[:, :], gnb_sb[:, :], Bv_sb[:, :], ALU.subtract,
             reads=[("c", "gnb_v"), ("Bv",)], writes=[("Bv",)])

        if sub2 < 7:
            pb.dma("sp", out_d[0:128, :], xst_sb[:, 0:1024],
                   reads=[("xst",)], writes=[("outz",)])
            pb.op("sp", lambda e: e.nop(), reads=[("outz",)], writes=())
            pb.finalize()
            _emit(nc, pb)
            return nc
        pb.barrier()
        load_weight(wA_sb, "wo_l", "wA")
        for blk in range(NPB):
            ybuf = yl_sb[blk % 2]
            pb.dma("act", ybuf[:, :], yd[blk, :, :],
                   reads=[("yd", blk), ("yl", blk % 2)], writes=[("yl", blk % 2)])
            yh = yb_sb[blk % 2]
            for ccn in range(NCC):
                v_ts("dve", yh[:, ccn * PBLK:(ccn + 1) * PBLK],
                     ybuf[:, ccn * PBLK:(ccn + 1) * PBLK], Av_sb[:, ccn:ccn + 1],
                     Bv_sb[:, ccn:ccn + 1], ALU.mult, ALU.add,
                     reads=[("yl", blk % 2), ("Av",), ("Bv",), ("yb", blk % 2)],
                     writes=[("yb", blk % 2)])
            for con in range(NCC):
                bk = bank(con % 2)
                for ccn in range(NCC):
                    t_mm(bk, PS[con % 2][:, 0:PBLK],
                         wA_sb[:, ccn * 1024 + con * 128: ccn * 1024 + (con + 1) * 128],
                         yh[:, ccn * PBLK:(ccn + 1) * PBLK],
                         reads=[("wA", ccn), ("yb", blk % 2)],
                         start=(ccn == 0), stop=(ccn == 7))
                xs = x_sb[:, con * TC + blk * PBLK: con * TC + (blk + 1) * PBLK]
                v_stt("dve", xs, PS[con % 2][:, 0:PBLK], ga_c[:, con:con + 1], xs,
                      ALU.mult, ALU.add,
                      reads=[bk, ("mt", 2), ("x", con)],
                      writes=[("x", con), ("x2", con, blk)])

        if "x2" in dbg:
            pb.dma("pool", dbg["x2"][:, :], x_sb[:, :],
                   reads=[("x2", ccn, blk) for ccn in range(NCC) for blk in range(NPB)],
                   writes=[("dbgx2",)])
            out_keys.append(("dbgx2",))
        if "y" in dbg:
            def fnd2d(e, o2=dbg["y"].ap(), i=yd.ap()):
                return e.dma_start(out=o2, in_=i)
            pb.op("pool", fnd2d, reads=[("yd", blk) for blk in range(NPB)],
                  writes=[("dbgy",)], amt=16, kind="dma")
            out_keys.append(("dbgy",))

    if phase_limit >= 3:
        pb.barrier()

        def x2key(ccn, blk):
            return ("x2", ccn, blk)
        ln_stats(x2key)

        def h2sl(ccn, blk):
            return h2_sb[:, ccn * TC + blk * PBLK: ccn * TC + (blk + 1) * PBLK]

        if sub4 < 1:
            pb.dma("sp", out_d[0:128, :], xst_sb[:, 0:1024],
                   reads=[("xst",)], writes=[("outz2",)])
            pb.op("sp", lambda e: e.nop(), reads=[("outz2",)], writes=())
            pb.finalize()
            _emit(nc, pb)
            return nc
        for blk in range(NPB):
            h_block(blk, x2key, h2sl, lambda c2, b2: ("h2", c2, b2), a2_c, sb_c, [("mt", 4), ("mt", 3)])
        for blk in range(NPB):
            for ccn in range(NCC):
                col = blk * 8 + ccn
                v_red("dve", hmax_sb[:, col:col + 1], h2sl(ccn, blk), ALU.max,
                      reads=[("h2", ccn, blk)], writes=[("hmaxc", col)], absval=True)
        if sub4 < 2:
            pb.dma("sp", out_d[0:128, :], xst_sb[:, 0:1024],
                   reads=[("xst",)], writes=[("outz2",)])
            pb.op("sp", lambda e: e.nop(), reads=[("outz2",)], writes=())
            pb.finalize()
            _emit(nc, pb)
            return nc
        v_red("dve", hmax1_sb[:, :], hmax_sb[:, :], ALU.max,
              reads=[("hmaxc", c2) for c2 in range(64)], writes=[("hmax1",)])
        if sub4 < 3:
            pb.dma("sp", out_d[0:128, :], xst_sb[:, 0:1024],
                   reads=[("xst",)], writes=[("outz2",)])
            pb.op("sp", lambda e: e.nop(), reads=[("outz2",)], writes=())
            pb.finalize()
            _emit(nc, pb)
            return nc
        t_mm(bank(4), PS[4][0:1, 0:128], hmax1_sb[:, 0:1], idf_sb[:, :],
             reads=[("hmax1",), ("c", "ident_f32")], start=True, stop=True, is_tr=True)
        v_red("dve", sA_sb[:, :], PS[4][0:1, 0:128], ALU.max,
              reads=[bank(4)], writes=[("sA",)])
        if sub4 < 4:
            pb.dma("sp", out_d[0:128, :], xst_sb[:, 0:1024],
                   reads=[("xst",)], writes=[("outz2",)])
            pb.op("sp", lambda e: e.nop(), reads=[("outz2",)], writes=())
            pb.finalize()
            _emit(nc, pb)
            return nc
        v_memset("dve", sc16_sb[:, :], 0.0, writes=[("sc16",)])
        v_copy("dve", sc16_sb[0:1, :], sA_sb[:, :], reads=[("sA",), ("sc16",)],
               writes=[("sc16",)])
        pb.dma("pool", mx1_in.ap(), sc16_sb[:, :], reads=[("sc16",)],
               writes=[("mx1in",)])
        pb.cc("ccmax1", "AllReduce", ALU.max, [list(range(8))],
              mx1_in.ap().opt(), mx1_out.ap().opt(),
              reads=[("mx1in",)], writes=[("mx1out",)])
        pb.dma("pool", sc16b_sb[:, :], mx1_out.ap(), reads=[("mx1out",), ("sc16b",)],
               writes=[("sc16b",)])
        a_act(sA_sb[:, :], clipx_sb[:, :], AF.Sigmoid,
              reads=[("c", "clipx"), ("sA",)], writes=[("sA",)])
        v_tt("dve", sA_sb[:, :], sA_sb[:, :], sc16b_sb[0:1, 0:1], ALU.mult,
             reads=[("sA",), ("sc16b",)], writes=[("sA",)])
        v_ts("dve", sA_sb[:, :], sA_sb[:, :], 1.0 / QMAX, None, ALU.mult, None,
             reads=[("sA",)], writes=[("sA",)])
        v_recip("dve", sB_sb[:, :], sA_sb[:, :], reads=[("sA",), ("sB",)],
                writes=[("sB",)])
        def bcast_scalar(src_key, src_ap, dst_ap, dst_key, bcol):
            t_mm(bank(5), PS[5][:, bcol:bcol + 1], onerf_sb[:, :], src_ap,
                 reads=[("c", "one_row_f32"), src_key], start=True, stop=True)
            v_copy("dve", dst_ap, PS[5][:, bcol:bcol + 1], reads=[bank(5), dst_key],
                   writes=[dst_key])

        if sub4 < 5:
            pb.dma("sp", out_d[0:128, :], xst_sb[:, 0:1024],
                   reads=[("xst",)], writes=[("outz2",)])
            pb.op("sp", lambda e: e.nop(), reads=[("outz2",)], writes=())
            pb.finalize()
            _emit(nc, pb)
            return nc
        bcast_scalar(("sB",), sB_sb[:, :], rB_sb[:, :], ("rB",), 0)
        bcast_scalar(("sA",), sA_sb[:, :], qsxB_sb[:, :], ("qsxB",), 1)
        if sub4 < 6:
            pb.dma("sp", out_d[0:128, :], xst_sb[:, 0:1024],
                   reads=[("xst",)], writes=[("outz2",)])
            pb.op("sp", lambda e: e.nop(), reads=[("outz2",)], writes=())
            pb.finalize()
            _emit(nc, pb)
            return nc
        for ccn in range(NCC):
            for blk in range(NPB):
                hs = h2sl(ccn, blk)
                vt = vst_sb[blk % 2]
                v_ts("dve", vt[:, :], hs, rB_sb[:, 0:1], RMAGIC, ALU.mult, ALU.add,
                     reads=[("h2", ccn, blk), ("rB",), ("vst", blk % 2)],
                     writes=[("vst", blk % 2)])
                v_ts("dve", hs, vt[:, :], RMAGIC, QMAX, ALU.subtract, ALU.min,
                     reads=[("vst", blk % 2), ("h2", ccn, blk)],
                     writes=[("h2", ccn, blk)])
                v_ts("dve", hs, hs, -QMAX - 1.0, None, ALU.max, None,
                     reads=[("h2", ccn, blk)], writes=[("h2", ccn, blk)])

        if "xq" in dbg:
            pb.dma("pool", dbg["xq"][:, :], h2_sb[:, :],
                   reads=[("h2", ccn, blk) for ccn in range(NCC) for blk in range(NPB)],
                   writes=[("dbgxq",)])
            out_keys.append(("dbgxq",))

    if phase_limit >= 4:
        pb.barrier()

        def quant_gu(nat_name, ich, dstslot, dstkey, svec, sveckey):
            pb.dma("sp", wnat_sb[:, :], d[nat_name][ich * 128:(ich + 1) * 128, :],
                   reads=[("wnat",)], writes=[("wnat",)])
            v_red("dve", wmx_sb[:, :], wnat_sb[:, :], ALU.max,
                  reads=[("wnat",), ("wmx",)], writes=[("wmx",)], absval=True)
            v_recip("dve", wri_sb[:, :], wmx_sb[:, :], reads=[("wmx",), ("wri",)],
                    writes=[("wri",)])
            v_ts("dve", wri_sb[:, :], wri_sb[:, :], QMAX, None, ALU.mult, None,
                 reads=[("wri",)], writes=[("wri",)])
            v_ts("dve", wsc_sb[:, :], wmx_sb[:, :], 1.0 / QMAX, None, ALU.mult, None,
                 reads=[("wmx",), ("wsc",)], writes=[("wsc",)])
            v_tt("dve", svec[:, ich:ich + 1], wsc_sb[:, :], qsxB_sb[:, 0:1], ALU.mult,
                 reads=[("wsc",), ("qsxB",)], writes=[(sveckey, ich)])
            v_ts("dve", wq1_sb[:, :], wnat_sb[:, :], wri_sb[:, 0:1], RMAGIC,
                 ALU.mult, ALU.add, reads=[("wnat",), ("wri",), ("wq1",)],
                 writes=[("wq1",)])
            v_ts("dve", wint_sb[:, :], wq1_sb[:, :], RMAGIC, QMAX,
                 ALU.subtract, ALU.min, reads=[("wq1",), ("wint",)],
                 writes=[("wint",)])
            v_ts("dve", wint_sb[:, :], wint_sb[:, :], -QMAX - 1.0, None, ALU.max,
                 None, reads=[("wint",)], writes=[("wint",)])
            bkt = ich % 2
            for j in range(NCC):
                t_mm(bankB(bkt), PSB[bkt][:, j * 128:(j + 1) * 128],
                     wint_sb[:, j * 128:(j + 1) * 128], idb_sb[:, :],
                     reads=[("wint",), ("c", "ident_bf")],
                     start=(j == 0), stop=(j == 7), is_tr=True)
            v_copy("dve", dstslot[:, :], PSB[bkt][:, :],
                   reads=[bankB(bkt), (dstkey, ich % 2)], writes=[(dstkey, ich % 2)])

        for ich in range(NI):
            quant_gu("wg_nat", ich, wig_sb[ich % 2], "wig", sgv_sb, "sg")
            quant_gu("wu_nat", ich, wiu_sb[ich % 2], "wiu", suv_sb, "su")
            for blk in range(NFB):
                bkg = bank((2 * blk) % 4)
                bku = bank((2 * blk + 1) % 4)
                for ccn in range(NCC):
                    xqs = h2_sb[:, ccn * TC + blk * FBLK: ccn * TC + (blk + 1) * FBLK]
                    t_mm(bkg, PS[(2 * blk) % 4][:, 0:FBLK],
                         wig_sb[ich % 2][:, ccn * 128:(ccn + 1) * 128], xqs,
                         reads=[("wig", ich % 2), ("h2", ccn, 2 * blk),
                                ("h2", ccn, 2 * blk + 1)],
                         start=(ccn == 0), stop=(ccn == 7))
                for ccn in range(NCC):
                    xqs = h2_sb[:, ccn * TC + blk * FBLK: ccn * TC + (blk + 1) * FBLK]
                    t_mm(bku, PS[(2 * blk + 1) % 4][:, 0:FBLK],
                         wiu_sb[ich % 2][:, ccn * 128:(ccn + 1) * 128], xqs,
                         reads=[("wiu", ich % 2), ("h2", ccn, 2 * blk),
                                ("h2", ccn, 2 * blk + 1)],
                         start=(ccn == 0), stop=(ccn == 7))
                gsl = gsl_sb[blk % 2]
                a_act(gsl[:, :], PS[(2 * blk) % 4][:, 0:FBLK], AF.Silu,
                      reads=[bkg, ("sg", ich), ("gsl", blk % 2)],
                      writes=[("gsl", blk % 2)], scale=sgv_sb[:, ich:ich + 1])
                ss = s_sb[:, ich * TC + blk * FBLK: ich * TC + (blk + 1) * FBLK]
                v_tt("dve", ss, PS[(2 * blk + 1) % 4][:, 0:FBLK], gsl[:, :], ALU.mult,
                     reads=[bku, ("gsl", blk % 2)], writes=[("s", ich, blk)])
                col = ich * NFB + blk
                v_red("dve", hmax1_sb[:, :], ss, ALU.max,
                      reads=[("s", ich, blk), ("hmax1",)], writes=[("hmax1",)],
                      absval=True)
                v_tt("dve", smax_sb[:, col:col + 1], hmax1_sb[:, :],
                     suv_sb[:, ich:ich + 1], ALU.mult,
                     reads=[("hmax1",), ("su", ich)], writes=[("smc", col)])

        v_red("dve", smax1_sb[:, :], smax_sb[:, :], ALU.max,
              reads=[("smc", c2) for c2 in range(NI * NFB)], writes=[("smax1",)])
        t_mm(bank(4), PS[4][0:1, 0:128], smax1_sb[:, 0:1], idf_sb[:, :],
             reads=[("smax1",), ("c", "ident_f32")], start=True, stop=True, is_tr=True)
        v_red("dve", sA_sb[:, :], PS[4][0:1, 0:128], ALU.max,
              reads=[bank(4), ("sA",)], writes=[("sA",)])
        v_copy("dve", sc16_sb[0:1, :], sA_sb[:, :], reads=[("sA",), ("sc16",)],
               writes=[("sc16",)])
        pb.dma("pool", mx2_in.ap(), sc16_sb[:, :], reads=[("sc16",)],
               writes=[("mx2in",)])
        pb.cc("ccmax2", "AllReduce", ALU.max, [list(range(8))],
              mx2_in.ap().opt(), mx2_out.ap().opt(),
              reads=[("mx2in",)], writes=[("mx2out",)])
        pb.dma("pool", sc16b_sb[:, :], mx2_out.ap(), reads=[("mx2out",), ("sc16b",)],
               writes=[("sc16b",)])
        a_act(sA_sb[:, :], clipd_sb[:, :], AF.Sigmoid,
              reads=[("c", "clipd"), ("sA",)], writes=[("sA",)])
        v_tt("dve", sA_sb[:, :], sA_sb[:, :], sc16b_sb[0:1, 0:1], ALU.mult,
             reads=[("sA",), ("sc16b",)], writes=[("sA",)])
        v_ts("dve", sA_sb[:, :], sA_sb[:, :], 1.0 / QMAX, None, ALU.mult, None,
             reads=[("sA",)], writes=[("sA",)])
        v_recip("dve", sB_sb[:, :], sA_sb[:, :], reads=[("sA",), ("sB",)],
                writes=[("sB",)])
        bcast_scalar(("sB",), sB_sb[:, :], rs2B_sb[:, :], ("rs2B",), 2)
        bcast_scalar(("sA",), sA_sb[:, :], qsdB_sb[:, :], ("qsdB",), 3)
        for ich in range(NI):
            v_tt("dve", sqc_sb[:, :], suv_sb[:, ich:ich + 1], rs2B_sb[:, 0:1],
                 ALU.mult, reads=[("su", ich), ("rs2B",), ("sqc",)], writes=[("sqc",)])
            srd = [("s", ich, b2) for b2 in range(NFB)]
            for hf2 in range(2):
                ss = s_sb[:, ich * TC + hf2 * 1024: ich * TC + (hf2 + 1) * 1024]
                sc2 = wnat_sb if hf2 == 0 else wq1_sb
                sck = ("wnat",) if hf2 == 0 else ("wq1",)
                v_ts("dve", sc2[:, :], ss, sqc_sb[:, 0:1], RMAGIC, ALU.mult,
                     ALU.add, reads=srd + [("sqc",), sck], writes=[sck])
                v_ts("dve", ss, sc2[:, :], RMAGIC, QMAX, ALU.subtract, ALU.min,
                     reads=[sck] + srd, writes=srd)
                v_ts("dve", ss, ss, -QMAX - 1.0, None, ALU.max, None,
                     reads=srd, writes=srd)

        if "s" in dbg:
            pb.dma("pool", dbg["s"][:, :], s_sb[:, :],
                   reads=[("s", ich, b2) for ich in range(NI) for b2 in range(NFB)],
                   writes=[("dbgs",)])
            out_keys.append(("dbgs",))

    if phase_limit >= 5:
        pb.barrier()

        for ccn in range(NCC):
            for hf in range(2):
                pb.dma("sp", wdn_sb[:, :],
                       d["wd_nat"][ccn * 128:(ccn + 1) * 128,
                                   hf * (IF // 2):(hf + 1) * (IF // 2)],
                       reads=[("wdnat",)], writes=[("wdnat",)])
                v_red("dve", wmx_sb[:, :] if hf == 0 else wsc_sb[:, :],
                      wdn_sb[:, :], ALU.max,
                      reads=[("wdnat",), ("wmx",) if hf == 0 else ("wsc",)],
                      writes=[("wmx",) if hf == 0 else ("wsc",)], absval=True)
            v_tt("dve", wmx_sb[:, :], wmx_sb[:, :], wsc_sb[:, :], ALU.max,
                 reads=[("wmx",), ("wsc",)], writes=[("wmx",)])
            v_recip("dve", wri_sb[:, :], wmx_sb[:, :], reads=[("wmx",), ("wri",)],
                    writes=[("wri",)])
            v_ts("dve", wri_sb[:, :], wri_sb[:, :], QMAX, None, ALU.mult, None,
                 reads=[("wri",)], writes=[("wri",)])
            v_ts("dve", wsc_sb[:, :], wmx_sb[:, :], 1.0 / QMAX, None, ALU.mult, None,
                 reads=[("wmx",), ("wsc",)], writes=[("wsc",)])
            v_tt("dve", wsc_sb[:, :], wsc_sb[:, :], qsdB_sb[:, 0:1], ALU.mult,
                 reads=[("wsc",), ("qsdB",)], writes=[("wsc",)])
            v_tt("dve", wsc_sb[:, :], wsc_sb[:, :], gb_c[:, ccn:ccn + 1], ALU.mult,
                 reads=[("wsc",), ("mt", 5)], writes=[("wsc",)])
            for hf in range(2):
                pb.dma("sp", wdn_sb[:, :],
                       d["wd_nat"][ccn * 128:(ccn + 1) * 128,
                                   hf * (IF // 2):(hf + 1) * (IF // 2)],
                       reads=[("wdnat",)], writes=[("wdnat",)])
                v_ts("dve", wdq1_sb[:, :], wdn_sb[:, :], wri_sb[:, 0:1], RMAGIC,
                     ALU.mult, ALU.add, reads=[("wdnat",), ("wri",), ("wdq1",)],
                     writes=[("wdq1",)])
                v_ts("dve", wdi_sb[:, :], wdq1_sb[:, :], RMAGIC, QMAX,
                     ALU.subtract, ALU.min, reads=[("wdq1",), ("wdint",)],
                     writes=[("wdint",)])
                v_ts("dve", wdi_sb[:, :], wdi_sb[:, :], -QMAX - 1.0, None, ALU.max,
                     None, reads=[("wdint",)], writes=[("wdint",)])
                v_ts("dve", wdi_sb[:, :], wdi_sb[:, :], wsc_sb[:, 0:1], None,
                     ALU.mult, None, reads=[("wdint",), ("wsc",)], writes=[("wdint",)])
                for jj in range(NI // 2):
                    ich = hf * (NI // 2) + jj
                    bkt = ich % 2
                    t_mm(bankB(bkt), PSB[bkt][:, 0:128],
                         wdi_sb[:, jj * 128:(jj + 1) * 128], idb_sb[:, :],
                         reads=[("wdint",), ("c", "ident_bf")],
                         start=True, stop=True, is_tr=True)
                    v_copy("dve",
                           wd_sb[:, ich * 1024 + ccn * 128: ich * 1024 + (ccn + 1) * 128],
                           PSB[bkt][:, 0:128],
                           reads=[bankB(bkt)], writes=[("wdop", ich, ccn)])

        for tcn in range(NRC):
            for hf in range(2):
                bk = bank((2 * tcn + hf) % 4)
                psd = PS[(2 * tcn + hf) % 4]
                for ich in range(NI):
                    t_mm(bk, psd[:, 0:512],
                         s_sb[:, ich * TC + tcn * 128: ich * TC + (tcn + 1) * 128],
                         wd_sb[:, ich * 1024 + hf * 512: ich * 1024 + (hf + 1) * 512],
                         reads=[("s", ich, tcn // 4)] +
                               [("wdop", ich, cc2) for cc2 in range(hf * 4, hf * 4 + 4)],
                         start=(ich == 0), stop=False)
                for k in range(4):
                    ccn = hf * 4 + k
                    xc = xc_sb[k % 2]
                    v_copy("dve", xc[:, :],
                           x_sb[:, ccn * TC + tcn * 128: ccn * TC + (tcn + 1) * 128],
                           reads=[("x2", ccn, tcn // 2), ("xc", k % 2)],
                           writes=[("xc", k % 2)])
                    t_mm(bk, psd[:, k * 128:(k + 1) * 128], xc[:, :], idf_sb[:, :],
                         reads=[("xc", k % 2), ("c", "ident_f32")],
                         start=False, stop=(k == 3), is_tr=True)
                a_act(outb_sb[:, hf * 512:(hf + 1) * 512], psd[:, 0:512], AF.Copy,
                      reads=[bk, ("outb",)], writes=[("outb",)])
            pb.dma("sp", out_d[tcn * 128:(tcn + 1) * 128, :], outb_sb[:, :],
                   reads=[("outb",)], writes=[("outd", tcn)])
            out_keys.append(("outd", tcn))
    else:
        pb.dma("sp", out_d[0:128, :], xst_sb[:, 0:1024],
               reads=[("xst",)], writes=[("outz",)])
        out_keys.append(("outz",))

    pb.op("sp", lambda e: e.nop(), reads=list(out_keys), writes=())

    pb.finalize()
    _emit(nc, pb)
    return nc


def _emit(nc, pb):
    import contextlib
    stack = contextlib.ExitStack()
    sems = {}
    for s in sorted(pb.sem_names):
        sems[s] = stack.enter_context(nc.semaphore(s))
    by_eng = {e: [] for e in ENGS}
    for o2 in pb.ops:
        by_eng[o2.eng].append(o2)

    def emit_stream(eng_obj, ops):
        for o2 in ops:
            for s, v in o2.waits:
                eng_obj.wait_ge(sems[s], v)
            inst = o2.fn(eng_obj)
            if o2.token is not None and inst is not None:
                inst.then_inc(sems[o2.token[0]], o2.amt)

    with nc.Block() as blk:
        @blk.tensor
        def _(e):
            emit_stream(e, by_eng["pe"])

        @blk.scalar
        def _(e):
            emit_stream(e, by_eng["act"])

        @blk.vector
        def _(e):
            emit_stream(e, by_eng["dve"])

        @blk.gpsimd
        def _(e):
            emit_stream(e, by_eng["pool"])

        @blk.sync
        def _(e):
            emit_stream(e, by_eng["sp"])
    stack.close()


_CACHED = {}

def kernel(**inputs):
    in_maps = _host_prepare(inputs)
    key = (tuple(DEBUG_TAPS), PHASE_LIMIT, SUB2, SUB3, SUB4)
    if key not in _CACHED:
        _CACHED[key] = build_program(DEBUG_TAPS, PHASE_LIMIT, SUB2, SUB3, SUB4)
    nc = _CACHED[key]
    res = run_bass_kernel_spmd(nc, in_maps, core_ids=list(range(8)))
    out = np.zeros((B, T, C), dtype=np.float32)
    for core in range(8):
        b, half = core // 2, core % 2
        out[b, half * TC:(half + 1) * TC, :] = res.results[core]["out"]
    kernel.last_results = res
    return out
